# version 01
# baseline (speedup 1.0000x reference)
from contextlib import ExitStack

import numpy as np
import ml_dtypes
import concourse.bass as bass
import concourse.mybir as mybir
from concourse.bass_utils import run_bass_kernel_spmd
import concourse.dve_ops as dve_ops
from concourse.dve_op import DveOp
from concourse.dve_spec import Spec, Src0, Src1, C0, C1, C2, lower as spec_lower

F32 = mybir.dt.float32
BF16 = mybir.dt.bfloat16
AF = mybir.ActivationFunctionType
ALU = mybir.AluOpType
AX = mybir.AxisListType

T = 4096
D = 1024
FF = 4096
I3 = 3072
N16 = 16
NCORES = 8
MAGIC = float(3 << 22)
NEG = -1.0e30
INV32 = 1.0 / 32.0


def _register_dve(name, spec, reference):
    if name in dve_ops._SUB_OPCODE_FOR_NAME:
        for op in dve_ops.OPS:
            if op.name == name:
                return op
    sha = {"v3": spec_lower(spec, ver="v3", perf_en=False, name=name).sha("v3")}
    op = DveOp(name, spec, subdim=False, uops_sha=sha, reference=reference)
    dve_ops.OPS.append(op)
    dve_ops._SUB_OPCODE_FOR_NAME[name] = dve_ops._CUSTOM_DVE_ROW_BASE + len(dve_ops.OPS) - 1
    dve_ops.CUSTOM_DVE_SPECS[name] = spec
    dve_ops.CUSTOM_DVE_REFS[name] = reference
    return op


ROUND_SCALE = _register_dve(
    "ROUND_SCALE",
    Spec(body=((Src0 * C0 + C2) - C2) * C1),
    lambda in0, in1, s0, s1, imm2: np.rint(in0 * s0) * s1,
)


def ts(i, size):
    return slice(i * size, (i + 1) * size)



ENGINES = ("sync", "vector", "scalar", "tensor", "gpsimd")


class Script:
    def __init__(self):
        self.steps = []

    def c(self, eng, fn):
        self.steps.append((eng, "c", fn, 1))

    def d(self, eng, fn, n=1):
        self.steps.append((eng, "d", fn, n))

    def run(self, nc):
        with ExitStack() as es:
            sems = {e: es.enter_context(nc.semaphore("sem_" + e)) for e in ENGINES}
            cnt = {e: 0 for e in ENGINES}
            plan = []
            for eng, kind, fn, n in self.steps:
                cnt[eng] += 1 if kind == "c" else 16 * n
                plan.append((eng, cnt[eng]))

            def section(e, engobj):
                for i, (eng, kind, fn, n) in enumerate(self.steps):
                    if eng != e:
                        continue
                    if i > 0:
                        peng, pval = plan[i - 1]
                        engobj.wait_ge(sems[peng], pval)
                    instrs = fn(engobj)
                    if not isinstance(instrs, (list, tuple)):
                        instrs = [instrs]
                    if kind == "c":
                        instrs[-1].then_inc(sems[e], 1)
                    else:
                        assert len(instrs) == n, f"step {i}: expected {n} dmas, got {len(instrs)}"
                        for ins_ in instrs:
                            ins_.then_inc(sems[e], 16)

            import os
            if os.environ.get("SCRIPT_VERBOSE"):
                from collections import Counter
                print("script steps:", len(self.steps), Counter(s[0] for s in self.steps))
            block = es.enter_context(nc.Block())

            @block.sync
            def _(sync):
                section("sync", sync)
                leng, lval = plan[-1]
                sync.wait_ge(sems[leng], lval)

            @block.vector
            def _(v):
                section("vector", v)

            @block.scalar
            def _(s):
                section("scalar", s)

            @block.tensor
            def _(te):
                section("tensor", te)

            @block.gpsimd
            def _(g):
                section("gpsimd", g)



def add_const_pool(nc, sc, pool, values):
    for i, v in enumerate(values):
        nc.const_aps.register(F32, float(v), pool[:, i:i + 1])

    def f(vec):
        ins = []
        for i, v in enumerate(values):
            ins.append(vec.memset(pool[:, i:i + 1], float(v)))
        return ins[-1]
    sc.c("vector", f)


def add_identity(sc, ident_bf, ident_f32=None):
    def f(g):
        ins = []
        g.memset(ident_bf[:], 0.0)
        ins.append(g.affine_select(
            out=ident_bf[:], in_=ident_bf[:], compare_op=ALU.not_equal,
            fill=1.0, base=0, pattern=[[-1, 128]], channel_multiplier=1))
        if ident_f32 is not None:
            g.memset(ident_f32[:], 0.0)
            ins.append(g.affine_select(
                out=ident_f32[:], in_=ident_f32[:], compare_op=ALU.not_equal,
                fill=1.0, base=0, pattern=[[-1, 128]], channel_multiplier=1))
        return ins[-1]
    sc.c("gpsimd", f)


def add_scalar_prep(sc, bufs, amax_in, qmax, clip_lo, mode):
    am8row, am, sc2, rsb, scr2 = (bufs["am8row"], bufs["am"], bufs["sc2"],
                                  bufs["rsb"], bufs["scr2"])

    def load(sync):
        return sync.dma_start(out=am8row[:], in_=amax_in.rearrange("a one -> one a"))
    sc.d("sync", load)

    def red(v):
        return v.reduce_max(am[:], am8row[:], axis=AX.X)
    sc.c("vector", red)

    if mode == "a16":
        sc.c("vector", lambda v: v.tensor_scalar(am[:], am[:], clip_lo, None, ALU.max))
        def mk(v):
            v.reciprocal(sc2[:, 0:1], am[:])
            return v.tensor_scalar(sc2[:, 1:2], am[:], 1.0 / qmax, None, ALU.mult)
        sc.c("vector", mk)
        sc.c("vector", lambda v: v.tensor_scalar(
            sc2[:, 0:1], sc2[:, 0:1], qmax, None, ALU.mult))
    else:
        sc.c("vector", lambda v: v.tensor_scalar(
            sc2[:, 1:2], am[:], 1.0 / qmax, clip_lo, ALU.mult, ALU.max))
        sc.c("vector", lambda v: v.reciprocal(sc2[:, 0:1], sc2[:, 1:2]))

    def store(sync):
        return sync.dma_start(out=scr2, in_=sc2[:])
    sc.d("sync", store)

    def bcast(sync):
        return sync.dma_start(out=rsb[:], in_=scr2.broadcast_to([128, 2]))
    sc.d("sync", bcast)


def add_amax_finalize(sc, amax_acc, amT, am1, am_bounce, amax_out):
    def st1(sync):
        return sync.dma_start(out=am_bounce.rearrange("(p one) -> p one", one=1),
                              in_=amax_acc[:])
    sc.d("sync", st1)

    def ld1(sync):
        return sync.dma_start(out=amT[:],
                              in_=am_bounce.rearrange("(one p) -> one p", one=1))
    sc.d("sync", ld1)

    def red(v):
        return v.reduce_max(am1[:], amT[:], axis=AX.X)
    sc.c("vector", red)

    def st2(sync):
        return sync.dma_start(out=amax_out, in_=am1[:])
    sc.d("sync", st2)



def build_l0_attn(dbg=False, dbg_qb=31):
    nc = bass.Bass("TRN2", target_bir_lowering=False, debug=False)
    x_tok = nc.dram_tensor("x_tok", [T, D], F32, kind="ExternalInput").ap()
    wqkv = nc.dram_tensor("wqkv", [D, 3 * D], F32, kind="ExternalInput").ap()
    x1T = nc.dram_tensor("x1T", [D, T], F32, kind="ExternalOutput").ap()
    amax1 = nc.dram_tensor("amax1", [1, 1], F32, kind="ExternalOutput").ap()
    ikind = "ExternalOutput" if dbg else "Internal"
    xt_bf = nc.dram_tensor("xt_bf", [D, T], BF16, kind=ikind).ap()
    wq_bf = nc.dram_tensor("wq_bf", [D, 3 * D], BF16, kind=ikind).ap()
    qt_bf = nc.dram_tensor("qt_bf", [D, T], BF16, kind=ikind).ap()
    am_bounce = nc.dram_tensor("am_bounce", [128], F32, kind="Internal").ap()
    if dbg:
        kT_d = nc.dram_tensor("kT_d", [128, 8 * T], BF16, kind="ExternalOutput").ap()
        vv_d = nc.dram_tensor("vv_d", [128, 32 * 1024], BF16, kind="ExternalOutput").ap()
        sc_d = nc.dram_tensor("sc_d", [128, T], F32, kind="ExternalOutput").ap()
        p_d = nc.dram_tensor("p_d", [128, T], BF16, kind="ExternalOutput").ap()
        y1_d = nc.dram_tensor("y1_d", [128, D], F32, kind="ExternalOutput").ap()
        at_d = nc.dram_tensor("at_d", [128, D], F32, kind="ExternalOutput").ap()
        sm_d = nc.dram_tensor("sm_d", [128, 16], F32, kind="ExternalOutput").ap()

    sc = Script()
    with ExitStack() as es:
        A = lambda n, sh, dt: es.enter_context(nc.sbuf_tensor(n, sh, dt))
        PS = lambda n, sh, dt: es.enter_context(nc.psum_tensor(n, sh, dt))
        stage_f = A("stage_f", [128, 1024], F32)
        stage_b = A("stage_b", [128, 1024], BF16)
        kT = A("kT", [128, 8 * T], BF16)
        vv = A("vv", [128, 32 * 1024], BF16)
        wvb = A("wvb", [128, 8 * 512], BF16)
        wqk_t = A("wqk_t", [128, 8 * 128], BF16)
        xc = A("xc", [128, 8 * 512], BF16)
        scb = A("scb", [128, T], F32)
        pbuf = A("pbuf", [128, T], BF16)
        pT2 = A("pT2", [128, 256], BF16)
        qTb = A("qTb", [128, 8 * 128], BF16)
        xrow = A("xrow", [128, 1024], F32)
        y1 = A("y1", [128, 1024], F32)
        sqv = A("sqv", [128, 1024], F32)
        x1r = A("x1r", [128, 1024], F32)
        x1tr = A("x1tr", [128, 1024], F32)
        sm = A("sm", [128, 16], F32)
        amax_acc = A("amax_acc", [128, 1], F32)
        amT = A("amT", [1, 128], F32)
        am1 = A("am1", [1, 1], F32)
        ident = A("ident", [128, 128], BF16)
        identf = A("identf", [128, 128], F32)
        ps_a = PS("ps_a", [128, 512], F32)
        ps_o = PS("ps_o", [128, 1024], F32)
        ps_tb = PS("ps_tb", [128, 128], BF16)
        ps_tf = PS("ps_tf", [128, 128], F32)

        COL = {k: i for i, k in enumerate(
            ["rowmax", "nbias", "lsum", "linv", "sumy", "sumsq", "m", "ey2",
             "mm", "var", "sd", "rstd", "mb", "negmb"])}

        def col(k):
            return sm[:, COL[k]:COL[k] + 1]

        cpool = A("cpool", [128, 2], F32)
        add_const_pool(nc, sc, cpool, [1e-5])
        add_identity(sc, ident, identf)
        sc.c("vector", lambda v: v.memset(amax_acc[:], 0.0))

        for r in range(8):
            for cc in range(3):
                sc.d("sync", lambda sync, r=r, cc=cc: sync.dma_start(
                    out=stage_f[:], in_=wqkv[ts(r, 128), ts(cc, 1024)]))
                sc.c("scalar", lambda s, r=r, cc=cc: s.activation(
                    out=stage_b[:], in_=stage_f[:], func=AF.Copy))
                sc.d("sync", lambda sync, r=r, cc=cc: sync.dma_start(
                    out=wq_bf[ts(r, 128), ts(cc, 1024)], in_=stage_b[:]))

        for i in range(32):
            sc.d("sync", lambda sync, i=i: sync.dma_start(
                out=xrow[:], in_=x_tok[ts(i, 128), :]))
            sc.c("scalar", lambda s, i=i: s.activation(
                out=stage_b[:], in_=xrow[:], func=AF.Copy))
            for j in range(8):
                sc.c("tensor", lambda te, j=j: te.transpose(
                    ps_tb[:], stage_b[:, ts(j, 128)], ident[:]))
                sc.c("vector", lambda v, j=j: v.tensor_copy(
                    xc.bitcast(BF16)[:, ts(j, 128)] if False else xc[:, ts(j, 128)], ps_tb[:]))
            sc.d("sync", lambda sync, i=i: [sync.dma_start(
                out=xt_bf[ts(j, 128), ts(i, 128)], in_=xc[:, ts(j, 128)])
                for j in range(8)], n=8)

        for dvh in range(2):
            sc.d("sync", lambda sync, dvh=dvh: [sync.dma_start(
                out=wvb[:, ts(db, 512)],
                in_=wq_bf[ts(db, 128), 2 * D + dvh * 512: 2 * D + dvh * 512 + 512])
                for db in range(8)], n=8)
            for tc in range(8):
                sc.d("sync", lambda sync, tc=tc: [sync.dma_start(
                    out=xc[:, ts(db, 512)], in_=xt_bf[ts(db, 128), ts(tc, 512)])
                    for db in range(8)], n=8)
                for sub in range(4):
                    def mmv(te, tc=tc, sub=sub, dvh=dvh):
                        ins = []
                        for db in range(8):
                            ins.append(te.matmul(
                                ps_a[:],
                                xc[:, db * 512 + sub * 128: db * 512 + sub * 128 + 128],
                                wvb[:, ts(db, 512)],
                                start=(db == 0), stop=(db == 7)))
                        return ins[-1]
                    sc.c("tensor", mmv)
                    kt = tc * 4 + sub
                    sc.c("scalar", lambda s, kt=kt, dvh=dvh: s.activation(
                        out=vv[:, kt * 1024 + dvh * 512: kt * 1024 + dvh * 512 + 512],
                        in_=ps_a[:], func=AF.Copy))

        for tc in range(8):
            sc.d("sync", lambda sync, tc=tc: [sync.dma_start(
                out=xc[:, ts(db, 512)], in_=xt_bf[ts(db, 128), ts(tc, 512)])
                for db in range(8)], n=8)
            for qk in range(2):
                for ob in range(8):
                    sc.d("sync", lambda sync, qk=qk, ob=ob: [sync.dma_start(
                        out=wqk_t[:, ts(db, 128)],
                        in_=wq_bf[ts(db, 128), qk * D + ob * 128: qk * D + ob * 128 + 128])
                        for db in range(8)], n=8)

                    def mmqk(te, tc=tc, qk=qk, ob=ob):
                        ins = []
                        for db in range(8):
                            ins.append(te.matmul(
                                ps_a[:], wqk_t[:, ts(db, 128)],
                                xc[:, ts(db, 512)],
                                start=(db == 0), stop=(db == 7)))
                        return ins[-1]
                    sc.c("tensor", mmqk)
                    if qk == 1:
                        sc.c("scalar", lambda s, ob=ob, tc=tc: s.activation(
                            out=kT[:, ob * T + tc * 512: ob * T + tc * 512 + 512],
                            in_=ps_a[:], func=AF.Copy))
                    else:
                        sc.c("scalar", lambda s: s.activation(
                            out=stage_b[:, 0:512], in_=ps_a[:], func=AF.Copy))
                        sc.d("sync", lambda sync, ob=ob, tc=tc: sync.dma_start(
                            out=qt_bf[ts(ob, 128), ts(tc, 512)], in_=stage_b[:, 0:512]))

        for qb in range(32):
            klen = (qb + 1) * 128
            nkc = (klen + 511) // 512
            sc.d("sync", lambda sync, qb=qb: [sync.dma_start(
                out=qTb[:, ts(db, 128)], in_=qt_bf[ts(db, 128), ts(qb, 128)])
                for db in range(8)], n=8)
            sc.d("sync", lambda sync, qb=qb: sync.dma_start(
                out=xrow[:], in_=x_tok[ts(qb, 128), :]))
            for kc in range(nkc):
                w = min(512, klen - kc * 512)

                def mms(te, qb=qb, kc=kc, w=w):
                    ins = []
                    for db in range(8):
                        ins.append(te.matmul(
                            ps_a[:, 0:w], qTb[:, ts(db, 128)],
                            kT[:, db * T + kc * 512: db * T + kc * 512 + w],
                            start=(db == 0), stop=(db == 7)))
                    return ins[-1]
                sc.c("tensor", mms)
                sc.c("scalar", lambda s, kc=kc, w=w: s.activation(
                    out=scb[:, kc * 512: kc * 512 + w], in_=ps_a[:, 0:w],
                    func=AF.Copy))
            sc.c("gpsimd", lambda g, qb=qb: g.affine_select(
                out=scb[:, ts(qb, 128)], in_=scb[:, ts(qb, 128)],
                compare_op=ALU.is_ge, fill=NEG, base=0,
                pattern=[[-1, 128]], channel_multiplier=1))
            sc.c("vector", lambda v, klen=klen: v.reduce_max(
                col("rowmax"), scb[:, 0:klen], axis=AX.X))
            sc.c("vector", lambda v: v.tensor_scalar(
                col("nbias"), col("rowmax"), -INV32, None, ALU.mult))
            sc.c("scalar", lambda s, klen=klen: s.activation(
                out=pbuf[:, 0:klen], in_=scb[:, 0:klen], func=AF.Exp,
                scale=INV32, bias=col("nbias"), accum_out=col("lsum")))
            sc.c("vector", lambda v: v.reciprocal(col("linv"), col("lsum")))
            for kt in range(qb + 1):
                sc.c("tensor", lambda te, kt=kt: te.transpose(
                    ps_tb[:], pbuf[:, ts(kt, 128)], ident[:]))
                sl = (kt % 2) * 128
                sc.c("vector", lambda v, sl=sl: v.tensor_copy(
                    pT2[:, sl:sl + 128], ps_tb[:]))

                def mmav(te, kt=kt, qb=qb, sl=sl):
                    ins = []
                    for dvh in range(2):
                        ins.append(te.matmul(
                            ps_o[:, ts(dvh, 512)], pT2[:, sl:sl + 128],
                            vv[:, kt * 1024 + dvh * 512: kt * 1024 + dvh * 512 + 512],
                            start=(kt == 0), stop=(kt == qb)))
                    return ins[-1]
                sc.c("tensor", mmav)
            if dbg and qb == dbg_qb:
                sc.c("vector", lambda v: v.tensor_scalar(
                    sqv[:], ps_o[:], col("linv"), None, ALU.mult))
                sc.d("sync", lambda sync: sync.dma_start(out=at_d, in_=sqv[:]))
                sc.d("sync", lambda sync: sync.dma_start(out=sc_d, in_=scb[:]))
                sc.d("sync", lambda sync: sync.dma_start(out=p_d, in_=pbuf[:]))
            sc.c("vector", lambda v: v.scalar_tensor_tensor(
                y1[:], ps_o[:], col("linv"), xrow[:], ALU.mult, ALU.add,
                accum_out=col("sumy")))
            sc.c("scalar", lambda s: s.activation(
                out=sqv[:], in_=y1[:], func=AF.Square, accum_out=col("sumsq")))

            def stats(v):
                v.tensor_scalar(col("m"), col("sumy"), 1.0 / D, None, ALU.mult)
                return v.tensor_scalar(col("ey2"), col("sumsq"), 1.0 / D, None, ALU.mult)
            sc.c("vector", stats)
            sc.c("vector", lambda v: v.tensor_tensor(
                col("mm"), col("m"), col("m"), ALU.mult))
            sc.c("vector", lambda v: v.tensor_tensor(
                col("var"), col("ey2"), col("mm"), ALU.subtract))
            sc.c("scalar", lambda s: s.activation(
                out=col("sd"), in_=col("var"), func=AF.Sqrt, bias=1e-5))
            sc.c("vector", lambda v: v.reciprocal(col("rstd"), col("sd")))
            sc.c("vector", lambda v: v.tensor_tensor(
                col("mb"), col("m"), col("rstd"), ALU.mult))
            sc.c("vector", lambda v: v.tensor_scalar(
                col("negmb"), col("mb"), -1.0, None, ALU.mult))
            sc.c("vector", lambda v: v.tensor_scalar(
                x1r[:], y1[:], col("rstd"), col("negmb"), ALU.mult, ALU.add))

            if dbg and qb == dbg_qb:
                sc.d("sync", lambda sync: sync.dma_start(out=y1_d, in_=y1[:]))
                sc.d("sync", lambda sync: sync.dma_start(out=sm_d, in_=sm[:]))

            sc.c("vector", lambda v: v.reduce_max(
                col("rowmax"), x1r[:], axis=AX.X, apply_absolute_value=True))
            sc.c("vector", lambda v: v.tensor_tensor(
                amax_acc[:], amax_acc[:], col("rowmax"), ALU.max))
            for j in range(8):
                sc.c("tensor", lambda te, j=j: te.transpose(
                    ps_tf[:], x1r[:, ts(j, 128)], identf[:]))
                sc.c("vector", lambda v, j=j: v.tensor_copy(
                    x1tr[:, ts(j, 128)], ps_tf[:]))
            sc.d("sync", lambda sync, qb=qb: [sync.dma_start(
                out=x1T[ts(j, 128), ts(qb, 128)], in_=x1tr[:, ts(j, 128)])
                for j in range(8)], n=8)

        if dbg:
            sc.d("sync", lambda sync: sync.dma_start(out=kT_d, in_=kT[:]))
            sc.d("sync", lambda sync: sync.dma_start(out=vv_d, in_=vv[:]))
        add_amax_finalize(sc, amax_acc, amT, am1, am_bounce, amax1)
        sc.run(nc)
    mybir.codegen_inst_isa_subclasses(nc)
    return nc



_BF = ml_dtypes.bfloat16


def add_ln_rows(sc, rows, ps_sum, ps_sq, ps_al, ps_be, ones_row1, inv_n, eps_bias):
    R = lambda i: rows[:, i * 512:(i + 1) * 512]
    sc.c("vector", lambda v: [v.tensor_copy(R(0), ps_sum[:]),
                              v.tensor_copy(R(1), ps_sq[:])][-1])
    sc.c("vector", lambda v: [
        v.tensor_scalar(R(2), R(0), inv_n, None, ALU.mult),
        v.tensor_scalar(R(3), R(1), inv_n, None, ALU.mult)][-1])
    sc.c("vector", lambda v: v.tensor_tensor(R(5), R(2), R(2), ALU.mult))
    sc.c("vector", lambda v: v.tensor_tensor(R(4), R(3), R(5), ALU.subtract))
    sc.c("scalar", lambda s: s.activation(
        out=R(3), in_=R(4), func=AF.Sqrt, bias=eps_bias))
    sc.c("vector", lambda v: v.reciprocal(R(4), R(3)))
    sc.c("vector", lambda v: v.tensor_tensor(R(5), R(2), R(4), ALU.mult))
    sc.c("vector", lambda v: v.tensor_scalar(R(5), R(5), -1.0, None, ALU.mult))

    def bcasts(te):
        te.matmul(ps_al[:], ones_row1[:], R(4), start=True, stop=True)
        return te.matmul(ps_be[:], ones_row1[:], R(5), start=True, stop=True)
    sc.c("tensor", bcasts)


def add_qdq_weight_lpbq(sc, nc, stage_f, wq_t, awb, sbv, rbv, w_hbm, rb, nblk):
    W = nblk * 32

    def load(sync):
        return sync.dma_start(out=stage_f[:, 0:W], in_=w_hbm[ts(rb, 128), :])
    sc.d("sync", load)
    sc.c("vector", lambda v: v.tensor_reduce(
        awb[:, 0:nblk], stage_f[:, 0:W].rearrange("p (a b) -> p a b", b=32),
        axis=AX.X, op=ALU.max, apply_absolute_value=True))
    sc.c("vector", lambda v: v.tensor_scalar(
        sbv[:, 0:nblk], awb[:, 0:nblk], 1.0 / 7.0, 1e-12, ALU.mult, ALU.max))
    sc.c("vector", lambda v: v.reciprocal(rbv[:, 0:nblk], sbv[:, 0:nblk]))

    def rounds(v):
        ins = []
        for b in range(nblk):
            ins.append(v._custom_dve(
                ROUND_SCALE, out=wq_t[:, ts(b, 32)], in0=stage_f[:, ts(b, 32)],
                s0=rbv[:, b:b + 1], s1=sbv[:, b:b + 1], imm2=MAGIC))
        return ins[-1]
    sc.c("vector", rounds)



def build_l0_ffn_a():
    nc = bass.Bass("TRN2", target_bir_lowering=False, debug=False)
    x1T = nc.dram_tensor("x1T", [D, T], F32, kind="ExternalInput").ap()
    amax8 = nc.dram_tensor("amax8", [8, 1], F32, kind="ExternalInput").ap()
    w_inT = nc.dram_tensor("w_inT", [FF, D], F32, kind="ExternalInput").ap()
    b_in_t = nc.dram_tensor("b_in_t", [128, 32], F32, kind="ExternalInput").ap()
    hidT = nc.dram_tensor("hidT", [FF, T], BF16, kind="ExternalOutput").ap()
    amax_h = nc.dram_tensor("amax_h", [1, 1], F32, kind="ExternalOutput").ap()
    scr2 = nc.dram_tensor("scr2", [1, 2], F32, kind="Internal").ap()
    am_bounce = nc.dram_tensor("am_bounce", [128], F32, kind="Internal").ap()

    sc = Script()
    with ExitStack() as es:
        A = lambda n, sh, dt: es.enter_context(nc.sbuf_tensor(n, sh, dt))
        PS = lambda n, sh, dt: es.enter_context(nc.psum_tensor(n, sh, dt))
        stage_f = A("stage_f", [128, 1024], F32)
        w_q_t = A("w_q_t", [128, 1024], BF16)
        w_in_q = A("w_in_q", [128, 8 * FF], BF16)
        xi = A("xi", [128, 8 * T], BF16)
        hb = A("hb", [128, 512], BF16)
        binb = A("binb", [128, 32], F32)
        sw = A("sw", [128, 32], F32)
        evsc = A("evsc", [128, 32], F32)
        aw = A("aw", [128, 1], F32)
        rw = A("rw", [128, 1], F32)
        aw2 = A("aw2", [128, 1], F32)
        hacc = A("hacc", [128, 1], F32)
        am8row = A("am8row", [1, 8], F32)
        am = A("am", [1, 1], F32)
        sc2 = A("sc2", [1, 2], F32)
        rsb = A("rsb", [128, 2], F32)
        amT = A("amT", [1, 128], F32)
        am1 = A("am1", [1, 1], F32)
        ident = A("ident", [128, 128], BF16)
        ps_m = PS("ps_m", [128, 512], F32)
        ps_tb = PS("ps_tb", [128, 128], BF16)

        add_identity(sc, ident)
        sc.c("vector", lambda v: v.memset(hacc[:], 0.0))
        add_scalar_prep(sc, dict(am8row=am8row, am=am, sc2=sc2, rsb=rsb,
                                 scr2=scr2), amax8, 32767.0, 1e-8, "a16")
        sc.d("sync", lambda sync: sync.dma_start(out=binb[:], in_=b_in_t))

        for fb in range(32):
            sc.d("sync", lambda sync, fb=fb: sync.dma_start(
                out=stage_f[:], in_=w_inT[ts(fb, 128), :]))
            sc.c("vector", lambda v: v.reduce_max(
                aw[:], stage_f[:], axis=AX.X, apply_absolute_value=True))
            sc.c("vector", lambda v: v.tensor_scalar(
                aw[:], aw[:], 1e-8, None, ALU.max))
            sc.c("vector", lambda v: v.reciprocal(rw[:], aw[:]))
            sc.c("vector", lambda v, fb=fb: v.tensor_scalar(
                sw[:, fb:fb + 1], aw[:], 1.0 / 127.0, None, ALU.mult))
            sc.c("vector", lambda v: v.tensor_scalar(
                rw[:], rw[:], 127.0, None, ALU.mult))
            sc.c("vector", lambda v: v._custom_dve(
                ROUND_SCALE, out=w_q_t[:], in0=stage_f[:], s0=rw[:, 0:1],
                s1=1.0, imm2=MAGIC))
            for j in range(8):
                sc.c("tensor", lambda te, j=j: te.transpose(
                    ps_tb[:], w_q_t[:, ts(j, 128)], ident[:]))
                sc.c("vector", lambda v, j=j, fb=fb: v.tensor_copy(
                    w_in_q[:, j * FF + fb * 128: j * FF + fb * 128 + 128], ps_tb[:]))
        sc.c("vector", lambda v: v.tensor_scalar(
            evsc[:], sw[:], rsb[:, 1:2], None, ALU.mult))

        for db in range(8):
            for tc in range(8):
                sc.d("sync", lambda sync, db=db, tc=tc: sync.dma_start(
                    out=stage_f[:, 0:512], in_=x1T[ts(db, 128), ts(tc, 512)]))
                sc.c("vector", lambda v, db=db, tc=tc: v._custom_dve(
                    ROUND_SCALE, out=xi[:, db * T + tc * 512: db * T + tc * 512 + 512],
                    in0=stage_f[:, 0:512], s0=rsb[:, 0:1], s1=1.0, imm2=MAGIC))

        for fb in range(32):
            for tc in range(8):
                def mmh(te, fb=fb, tc=tc):
                    ins = []
                    for db in range(8):
                        ins.append(te.matmul(
                            ps_m[:],
                            w_in_q[:, db * FF + fb * 128: db * FF + fb * 128 + 128],
                            xi[:, db * T + tc * 512: db * T + tc * 512 + 512],
                            start=(db == 0), stop=(db == 7)))
                    return ins[-1]
                sc.c("tensor", mmh)
                sc.c("scalar", lambda s, fb=fb: s.activation(
                    out=hb[:], in_=ps_m[:], func=AF.Gelu,
                    scale=evsc[:, fb:fb + 1], bias=binb[:, fb:fb + 1]))
                sc.c("vector", lambda v: v.reduce_max(
                    aw2[:], hb[:], axis=AX.X, apply_absolute_value=True))
                sc.c("vector", lambda v: v.tensor_tensor(
                    hacc[:], hacc[:], aw2[:], ALU.max))
                sc.d("sync", lambda sync, fb=fb, tc=tc: sync.dma_start(
                    out=hidT[ts(fb, 128), ts(tc, 512)], in_=hb[:]))

        add_amax_finalize(sc, hacc, amT, am1, am_bounce, amax_h)
        sc.run(nc)
    mybir.codegen_inst_isa_subclasses(nc)
    return nc



def build_l0_ffn_b():
    nc = bass.Bass("TRN2", target_bir_lowering=False, debug=False)
    x1T = nc.dram_tensor("x1T", [D, T], F32, kind="ExternalInput").ap()
    hidT = nc.dram_tensor("hidT", [FF, T], BF16, kind="ExternalInput").ap()
    amax8 = nc.dram_tensor("amax8", [8, 1], F32, kind="ExternalInput").ap()
    w_out = nc.dram_tensor("w_out", [FF, D], F32, kind="ExternalInput").ap()
    b_out_t = nc.dram_tensor("b_out_t", [128, 8], F32, kind="ExternalInput").ap()
    x2T = nc.dram_tensor("x2T", [D, T], F32, kind="ExternalOutput").ap()
    scr2 = nc.dram_tensor("scr2", [1, 2], F32, kind="Internal").ap()

    sc = Script()
    with ExitStack() as es:
        A = lambda n, sh, dt: es.enter_context(nc.sbuf_tensor(n, sh, dt))
        PS = lambda n, sh, dt: es.enter_context(nc.psum_tensor(n, sh, dt))
        stage_f = A("stage_f", [128, 1024], F32)
        w_out_q = A("w_out_q", [128, 32 * 1024], BF16)
        hop = A("hop", [128, 32 * 512], BF16)
        yr = A("yr", [128, 8 * 512], F32)
        xtile = A("xtile", [128, 512], F32)
        sq = A("sq", [128, 512], F32)
        x2t = A("x2t", [128, 512], F32)
        tprod = A("tprod", [128, 512], F32)
        bout = A("bout", [128, 8], F32)
        sw = A("sw", [128, 32], F32)
        cf = A("cf", [128, 32], F32)
        aw = A("aw", [128, 1], F32)
        rw = A("rw", [128, 1], F32)
        am8row = A("am8row", [1, 8], F32)
        am = A("am", [1, 1], F32)
        sc2 = A("sc2", [1, 2], F32)
        rsb = A("rsb", [128, 2], F32)
        lnrows = A("lnrows", [1, 512 * 6], F32)
        ones_col = A("ones_col", [128, 1], F32)
        ones_row1 = A("ones_row1", [1, 128], F32)
        cpool = A("cpool", [128, 1], F32)
        ps_m = PS("ps_m", [128, 512], F32)
        ps_sum = PS("ps_sum", [1, 512], F32)
        ps_sq = PS("ps_sq", [1, 512], F32)
        ps_al = PS("ps_al", [128, 512], F32)
        ps_be = PS("ps_be", [128, 512], F32)

        add_const_pool(nc, sc, cpool, [1e-5])
        sc.c("vector", lambda v: v.memset(ones_col[:], 1.0))
        sc.c("vector", lambda v: v.memset(ones_row1[:], 1.0))
        add_scalar_prep(sc, dict(am8row=am8row, am=am, sc2=sc2, rsb=rsb,
                                 scr2=scr2), amax8, 32767.0, 1e-8, "a16")
        sc.d("sync", lambda sync: sync.dma_start(out=bout[:], in_=b_out_t))

        for fb in range(32):
            sc.d("sync", lambda sync, fb=fb: sync.dma_start(
                out=stage_f[:], in_=w_out[ts(fb, 128), :]))
            sc.c("vector", lambda v: v.reduce_max(
                aw[:], stage_f[:], axis=AX.X, apply_absolute_value=True))
            sc.c("vector", lambda v: v.tensor_scalar(
                aw[:], aw[:], 1e-8, None, ALU.max))
            sc.c("vector", lambda v: v.reciprocal(rw[:], aw[:]))
            sc.c("vector", lambda v, fb=fb: v.tensor_scalar(
                sw[:, fb:fb + 1], aw[:], 1.0 / 127.0, None, ALU.mult))
            sc.c("vector", lambda v: v.tensor_scalar(
                rw[:], rw[:], 127.0, None, ALU.mult))
            sc.c("vector", lambda v, fb=fb: v._custom_dve(
                ROUND_SCALE, out=w_out_q[:, ts(fb, 1024)], in0=stage_f[:],
                s0=rw[:, 0:1], s1=1.0, imm2=MAGIC))
        sc.c("vector", lambda v: v.tensor_scalar(
            cf[:], sw[:], rsb[:, 1:2], None, ALU.mult))

        for tc in range(8):
            for fb in range(32):
                sc.d("sync", lambda sync, fb=fb, tc=tc: sync.dma_start(
                    out=hop[:, ts(fb, 512)], in_=hidT[ts(fb, 128), ts(tc, 512)]))
                sc.c("vector", lambda v, fb=fb: v._custom_dve(
                    ROUND_SCALE, out=hop[:, ts(fb, 512)], in0=hop[:, ts(fb, 512)],
                    s0=rsb[:, 0:1], s1=cf[:, fb:fb + 1], imm2=MAGIC))
            for do in range(8):
                def mmo(te, do=do):
                    ins = []
                    for fb in range(32):
                        ins.append(te.matmul(
                            ps_m[:],
                            w_out_q[:, fb * 1024 + do * 128: fb * 1024 + do * 128 + 128],
                            hop[:, ts(fb, 512)],
                            start=(fb == 0), stop=(fb == 31)))
                    return ins[-1]
                sc.c("tensor", mmo)
                sc.d("sync", lambda sync, do=do, tc=tc: sync.dma_start(
                    out=xtile[:], in_=x1T[ts(do, 128), ts(tc, 512)]))
                sc.c("vector", lambda v, do=do: v.scalar_tensor_tensor(
                    yr[:, ts(do, 512)], ps_m[:], bout[:, do:do + 1], xtile[:],
                    ALU.add, ALU.add))
                sc.c("scalar", lambda s, do=do: s.activation(
                    out=sq[:], in_=yr[:, ts(do, 512)], func=AF.Square))

                def mmstat(te, do=do):
                    i1 = te.matmul(ps_sum[:], ones_col[:], yr[:, ts(do, 512)],
                                   start=(do == 0), stop=(do == 7))
                    i2 = te.matmul(ps_sq[:], ones_col[:], sq[:],
                                   start=(do == 0), stop=(do == 7))
                    return [i1, i2][-1]
                sc.c("tensor", mmstat)
            add_ln_rows(sc, lnrows, ps_sum, ps_sq, ps_al, ps_be, ones_row1,
                        1.0 / D, 1e-5)
            for do in range(8):
                sc.c("vector", lambda v, do=do: v.tensor_tensor(
                    tprod[:], yr[:, ts(do, 512)], ps_al[:], ALU.mult))
                sc.c("vector", lambda v: v.tensor_tensor(
                    x2t[:], tprod[:], ps_be[:], ALU.add))
                sc.d("sync", lambda sync, do=do, tc=tc: sync.dma_start(
                    out=x2T[ts(do, 128), ts(tc, 512)], in_=x2t[:]))
        sc.run(nc)
    mybir.codegen_inst_isa_subclasses(nc)
    return nc



def build_ssm(phases="123"):
    nc = bass.Bass("TRN2", target_bir_lowering=False, debug=False)
    x2T = nc.dram_tensor("x2T", [D, T], F32, kind="ExternalInput").ap()
    a_log = nc.dram_tensor("a_log", [D, N16], F32, kind="ExternalInput").ap()
    w_delta = nc.dram_tensor("w_delta", [D, D], F32, kind="ExternalInput").ap()
    b_delta_t = nc.dram_tensor("b_delta_t", [128, 8], F32, kind="ExternalInput").ap()
    w_b = nc.dram_tensor("w_b", [D, N16], F32, kind="ExternalInput").ap()
    w_c = nc.dram_tensor("w_c", [D, N16], F32, kind="ExternalInput").ap()
    dskip_t = nc.dram_tensor("dskip_t", [128, 8], F32, kind="ExternalInput").ap()
    nw_t = nc.dram_tensor("nw_t", [128, 8], F32, kind="ExternalInput").ap()
    x3T = nc.dram_tensor("x3T", [D, T], F32, kind="ExternalOutput").ap()
    amax3 = nc.dram_tensor("amax3", [1, 1], F32, kind="ExternalOutput").ap()
    xn_hbm = nc.dram_tensor("xn_hbm", [D, T], F32, kind="Internal").ap()
    xnb_hbm = nc.dram_tensor("xnb_hbm", [D, T], BF16, kind="Internal").ap()
    dl_hbm = nc.dram_tensor("dl_hbm", [D, T], F32, kind="Internal").ap()
    bp_hbm = nc.dram_tensor("bp_hbm", [N16, T], F32, kind="Internal").ap()
    c_hbm = nc.dram_tensor("c_hbm", [N16, T], F32, kind="Internal").ap()
    am_bounce = nc.dram_tensor("am_bounce", [128], F32, kind="Internal").ap()

    DT_MAX, DT_MIN = 10.0, 1e-4
    CEN = (DT_MAX + DT_MIN) / 2.0
    HALF = (DT_MAX - DT_MIN) / 2.0

    sc = Script()
    with ExitStack() as es:
        A = lambda n, sh, dt: es.enter_context(nc.sbuf_tensor(n, sh, dt))
        PS = lambda n, sh, dt: es.enter_context(nc.psum_tensor(n, sh, dt))
        stage_f = A("stage_f", [128, 1024], F32)
        wdel_b = A("wdel_b", [128, 8 * 1024], BF16)
        wb_b = A("wb_b", [128, 8 * N16], BF16)
        wc_b = A("wc_b", [128, 8 * N16], BF16)
        ahalf = A("ahalf", [128, 8 * N16], F32)
        aexp = A("aexp", [128, N16], F32)
        bdel = A("bdel", [128, 8], F32)
        dskip = A("dskip", [128, 8], F32)
        nww = A("nww", [128, 8], F32)
        xch = A("xch", [128, 8 * 512], F32)
        xnch = A("xnch", [128, 8 * 512], F32)
        dlch = A("dlch", [128, 8 * 512], F32)
        uch = A("uch", [128, 8 * 512], F32)
        ytc = A("ytc", [128, 8 * 512], F32)
        yrb = A("yrb", [128, 8 * 512], F32)
        sq = A("sq", [128, 512], F32)
        x3t = A("x3t", [128, 512], F32)
        tprod = A("tprod", [128, 512], F32)
        xbf = A("xbf", [128, 512], BF16)
        rhs_bf = A("rhs_bf", [128, 8 * 512], BF16)
        t1 = A("t1", [128, 512], F32)
        t2 = A("t2", [128, 512], F32)
        t3 = A("t3", [128, 512], F32)
        bp1 = A("bp1", [128, 512], F32)
        c1 = A("c1", [128, 512], F32)
        hn = A("hn", [128, 512], F32)
        hst = A("hst", [128, 128], F32)
        bprow = A("bprow", [16, 512], F32)
        lnrows = A("lnrows", [1, 512 * 6], F32)
        ones_col = A("ones_col", [128, 1], F32)
        ones_row1 = A("ones_row1", [1, 128], F32)
        aw2 = A("aw2", [128, 1], F32)
        amax_acc = A("amax_acc", [128, 1], F32)
        amT = A("amT", [1, 128], F32)
        am1 = A("am1", [1, 1], F32)
        cpool = A("cpool", [128, 4], F32)
        ps_m = PS("ps_m", [128, 512], F32)
        ps_bp = PS("ps_bp", [16, 512], F32)
        ps_sum = PS("ps_sum", [1, 512], F32)
        ps_sq = PS("ps_sq", [1, 512], F32)
        ps_al = PS("ps_al", [128, 512], F32)
        ps_be = PS("ps_be", [128, 512], F32)

        add_const_pool(nc, sc, cpool, [1e-5, 1e-6, 1.0, -CEN / HALF])
        sc.c("vector", lambda v: v.memset(ones_col[:], 1.0))
        sc.c("vector", lambda v: v.memset(ones_row1[:], 1.0))
        sc.c("vector", lambda v: v.memset(amax_acc[:], 0.0))
        sc.c("vector", lambda v: v.memset(hst[:], 0.0))
        sc.d("sync", lambda sync: [sync.dma_start(out=bdel[:], in_=b_delta_t),
                                   sync.dma_start(out=dskip[:], in_=dskip_t),
                                   sync.dma_start(out=nww[:], in_=nw_t)], n=3)
        for db in range(8):
            sc.d("sync", lambda sync, db=db: sync.dma_start(
                out=aexp[:], in_=a_log[ts(db, 128), :]))
            sc.c("scalar", lambda s: s.activation(
                out=aexp[:], in_=aexp[:], func=AF.Exp))
            sc.c("vector", lambda v, db=db: v.tensor_scalar(
                ahalf[:, ts(db, N16)], aexp[:], 0.5, None, ALU.mult))
        for db in range(8):
            sc.d("sync", lambda sync, db=db: sync.dma_start(
                out=stage_f[:], in_=w_delta[ts(db, 128), :]))
            sc.c("scalar", lambda s, db=db: s.activation(
                out=wdel_b[:, ts(db, 1024)], in_=stage_f[:], func=AF.Copy))
            sc.d("sync", lambda sync, db=db: sync.dma_start(
                out=stage_f[:, 0:N16], in_=w_b[ts(db, 128), :]))
            sc.c("scalar", lambda s, db=db: s.activation(
                out=wb_b[:, ts(db, N16)], in_=stage_f[:, 0:N16], func=AF.Copy))
            sc.d("sync", lambda sync, db=db: sync.dma_start(
                out=stage_f[:, 0:N16], in_=w_c[ts(db, 128), :]))
            sc.c("scalar", lambda s, db=db: s.activation(
                out=wc_b[:, ts(db, N16)], in_=stage_f[:, 0:N16], func=AF.Copy))

        for tc in (range(8) if "1" in phases else []):
            sc.d("sync", lambda sync, tc=tc: [sync.dma_start(
                out=xch[:, ts(db, 512)], in_=x2T[ts(db, 128), ts(tc, 512)])
                for db in range(8)], n=8)
            for db in range(8):
                sc.c("scalar", lambda s, db=db: s.activation(
                    out=sq[:], in_=xch[:, ts(db, 512)], func=AF.Square))
                sc.c("tensor", lambda te, db=db: te.matmul(
                    ps_sum[:], ones_col[:], sq[:],
                    start=(db == 0), stop=(db == 7)))
            sc.c("vector", lambda v: v.tensor_copy(lnrows[:, 0:512], ps_sum[:]))
            sc.c("vector", lambda v: v.tensor_scalar(
                lnrows[:, 512:1024], lnrows[:, 0:512], 1.0 / D, None, ALU.mult))
            sc.c("scalar", lambda s: s.activation(
                out=lnrows[:, 0:512], in_=lnrows[:, 512:1024], func=AF.Sqrt,
                bias=1e-6))
            sc.c("vector", lambda v: v.reciprocal(
                lnrows[:, 512:1024], lnrows[:, 0:512]))
            sc.c("tensor", lambda te: te.matmul(
                ps_al[:], ones_row1[:], lnrows[:, 512:1024], start=True, stop=True))
            for db in range(8):
                sc.c("vector", lambda v, db=db: v.scalar_tensor_tensor(
                    xnch[:, ts(db, 512)], xch[:, ts(db, 512)], nww[:, db:db + 1],
                    ps_al[:], ALU.mult, ALU.mult))
                sc.c("scalar", lambda s, db=db: s.activation(
                    out=xbf[:], in_=xnch[:, ts(db, 512)], func=AF.Copy))
                sc.d("sync", lambda sync, db=db, tc=tc: [
                    sync.dma_start(out=xn_hbm[ts(db, 128), ts(tc, 512)],
                                   in_=xnch[:, ts(db, 512)]),
                    sync.dma_start(out=xnb_hbm[ts(db, 128), ts(tc, 512)],
                                   in_=xbf[:])], n=2)

        for ob in (range(8) if "2" in phases else []):
            for tc in range(8):
                sc.d("sync", lambda sync, tc=tc: [sync.dma_start(
                    out=rhs_bf[:, ts(db, 512)],
                    in_=xnb_hbm[ts(db, 128), ts(tc, 512)]) for db in range(8)], n=8)

                def mmd(te, ob=ob):
                    ins = []
                    for db in range(8):
                        ins.append(te.matmul(
                            ps_m[:],
                            wdel_b[:, db * 1024 + ob * 128: db * 1024 + ob * 128 + 128],
                            rhs_bf[:, ts(db, 512)],
                            start=(db == 0), stop=(db == 7)))
                    return ins[-1]
                sc.c("tensor", mmd)
                sc.c("vector", lambda v, ob=ob: v.tensor_scalar(
                    t1[:], ps_m[:], bdel[:, ob:ob + 1], None, ALU.add))
                sc.c("scalar", lambda s: s.activation(
                    out=t2[:], in_=t1[:], func=AF.Abs))
                sc.c("scalar", lambda s: s.activation(
                    out=t2[:], in_=t2[:], func=AF.Exp, scale=-1.0))
                sc.c("scalar", lambda s: s.activation(
                    out=t2[:], in_=t2[:], func=AF.Ln, bias=1.0))
                sc.c("vector", lambda v: v.tensor_scalar(
                    t3[:], t1[:], 0.0, None, ALU.max))
                sc.c("vector", lambda v: v.tensor_tensor(
                    t2[:], t2[:], t3[:], ALU.add))
                sc.c("scalar", lambda s: s.activation(
                    out=t3[:], in_=t2[:], func=AF.Tanh, scale=1.0 / HALF,
                    bias=-CEN / HALF))
                sc.c("vector", lambda v: v.tensor_scalar(
                    t3[:], t3[:], HALF, CEN, ALU.mult, ALU.add))
                sc.d("sync", lambda sync, ob=ob, tc=tc: sync.dma_start(
                    out=dl_hbm[ts(ob, 128), ts(tc, 512)], in_=t3[:]))
        for tc in (range(8) if "2" in phases else []):
            sc.d("sync", lambda sync, tc=tc: [sync.dma_start(
                out=rhs_bf[:, ts(db, 512)],
                in_=xnb_hbm[ts(db, 128), ts(tc, 512)]) for db in range(8)], n=8)
            for wsb, hbm in ((wb_b, bp_hbm), (wc_b, c_hbm)):
                def mmb(te, wsb=wsb):
                    ins = []
                    for db in range(8):
                        ins.append(te.matmul(
                            ps_bp[:], wsb[:, ts(db, N16)], rhs_bf[:, ts(db, 512)],
                            start=(db == 0), stop=(db == 7)))
                    return ins[-1]
                sc.c("tensor", mmb)
                sc.c("vector", lambda v: v.tensor_copy(bprow[:], ps_bp[:]))
                sc.d("sync", lambda sync, hbm=hbm, tc=tc: sync.dma_start(
                    out=hbm[:, ts(tc, 512)], in_=bprow[:]))

        for tc in (range(8) if "3" in phases else []):
            sc.d("sync", lambda sync, tc=tc: [sync.dma_start(
                out=dlch[:, ts(db, 512)], in_=dl_hbm[ts(db, 128), ts(tc, 512)])
                for db in range(8)] + [sync.dma_start(
                out=xnch[:, ts(db, 512)], in_=xn_hbm[ts(db, 128), ts(tc, 512)])
                for db in range(8)], n=16)
            for db in range(8):
                sc.c("scalar", lambda s, db=db: s.activation(
                    out=sq[:], in_=dlch[:, ts(db, 512)], func=AF.Sqrt))
                sc.c("vector", lambda v, db=db: v.scalar_tensor_tensor(
                    uch[:, ts(db, 512)], sq[:], 0.5, xnch[:, ts(db, 512)],
                    ALU.mult, ALU.mult))
            for n in range(N16):
                sc.d("sync", lambda sync, n=n, tc=tc: [
                    sync.dma_start(out=bp1[:], in_=bp_hbm[n:n + 1, ts(tc, 512)]
                                   .broadcast_to([128, 512])),
                    sync.dma_start(out=c1[:], in_=c_hbm[n:n + 1, ts(tc, 512)]
                                   .broadcast_to([128, 512]))], n=2)
                for db in range(8):
                    hcol = hst[:, db * N16 + n: db * N16 + n + 1]
                    sc.c("vector", lambda v, db=db, n=n: v.tensor_scalar(
                        t1[:], dlch[:, ts(db, 512)],
                        ahalf[:, db * N16 + n: db * N16 + n + 1], 1.0,
                        ALU.mult, ALU.add))
                    sc.c("vector", lambda v: v.reciprocal_approx_fast(
                        out=t2[:], in_=t1[:]))
                    sc.c("vector", lambda v: v.tensor_scalar(
                        t1[:], t2[:], 2.0, -1.0, ALU.mult, ALU.add))
                    sc.c("vector", lambda v, db=db: v.tensor_tensor(
                        t3[:], uch[:, ts(db, 512)], bp1[:], ALU.mult))
                    sc.c("vector", lambda v: v.scalar_tensor_tensor(
                        t2[:], t1[:], 1.0, t3[:], ALU.add, ALU.mult))
                    sc.c("vector", lambda v, hcol=hcol: v.tensor_tensor_scan(
                        hn[:], t1[:], t2[:], hcol, ALU.mult, ALU.add))
                    sc.c("vector", lambda v, hcol=hcol: v.tensor_copy(
                        hcol, hn[:, 511:512]))
                    sc.c("vector", lambda v: v.tensor_tensor(
                        t3[:], hn[:], c1[:], ALU.mult))
                    if n == 0:
                        sc.c("vector", lambda v, db=db: v.tensor_copy(
                            ytc[:, ts(db, 512)], t3[:]))
                    else:
                        sc.c("vector", lambda v, db=db: v.tensor_tensor(
                            ytc[:, ts(db, 512)], ytc[:, ts(db, 512)], t3[:],
                            ALU.add))
            sc.d("sync", lambda sync, tc=tc: [sync.dma_start(
                out=xch[:, ts(db, 512)], in_=x2T[ts(db, 128), ts(tc, 512)])
                for db in range(8)], n=8)
            for db in range(8):
                sc.c("vector", lambda v, db=db: v.scalar_tensor_tensor(
                    t1[:], xnch[:, ts(db, 512)], dskip[:, db:db + 1],
                    ytc[:, ts(db, 512)], ALU.mult, ALU.add))
                sc.c("vector", lambda v, db=db: v.tensor_tensor(
                    yrb[:, ts(db, 512)], t1[:], xch[:, ts(db, 512)], ALU.add))
                sc.c("scalar", lambda s, db=db: s.activation(
                    out=sq[:], in_=yrb[:, ts(db, 512)], func=AF.Square))

                def mmstat(te, db=db):
                    te.matmul(ps_sum[:], ones_col[:], yrb[:, ts(db, 512)],
                              start=(db == 0), stop=(db == 7))
                    return te.matmul(ps_sq[:], ones_col[:], sq[:],
                                     start=(db == 0), stop=(db == 7))
                sc.c("tensor", mmstat)
            add_ln_rows(sc, lnrows, ps_sum, ps_sq, ps_al, ps_be, ones_row1,
                        1.0 / D, 1e-5)
            for db in range(8):
                sc.c("vector", lambda v, db=db: v.tensor_tensor(
                    tprod[:], yrb[:, ts(db, 512)], ps_al[:], ALU.mult))
                sc.c("vector", lambda v: v.tensor_tensor(
                    x3t[:], tprod[:], ps_be[:], ALU.add))
                sc.c("vector", lambda v: v.reduce_max(
                    aw2[:], x3t[:], axis=AX.X, apply_absolute_value=True))
                sc.c("vector", lambda v: v.tensor_tensor(
                    amax_acc[:], amax_acc[:], aw2[:], ALU.max))
                sc.d("sync", lambda sync, db=db, tc=tc: sync.dma_start(
                    out=x3T[ts(db, 128), ts(tc, 512)], in_=x3t[:]))

        add_amax_finalize(sc, amax_acc, amT, am1, am_bounce, amax3)
        sc.run(nc)
    mybir.codegen_inst_isa_subclasses(nc)
    return nc



def build_l1_ffn_a():
    nc = bass.Bass("TRN2", target_bir_lowering=False, debug=False)
    x3T = nc.dram_tensor("x3T", [D, T], F32, kind="ExternalInput").ap()
    amax8 = nc.dram_tensor("amax8", [8, 1], F32, kind="ExternalInput").ap()
    w_up = nc.dram_tensor("w_up", [I3, D], F32, kind="ExternalInput").ap()
    w_gate = nc.dram_tensor("w_gate", [I3, D], F32, kind="ExternalInput").ap()
    upT = nc.dram_tensor("upT", [I3, T], BF16, kind="ExternalOutput").ap()
    gateT = nc.dram_tensor("gateT", [I3, T], BF16, kind="ExternalOutput").ap()
    amax_u = nc.dram_tensor("amax_u", [1, 1], F32, kind="ExternalOutput").ap()
    amax_g = nc.dram_tensor("amax_g", [1, 1], F32, kind="ExternalOutput").ap()
    scr2 = nc.dram_tensor("scr2", [1, 2], F32, kind="Internal").ap()
    am_bounce = nc.dram_tensor("am_bounce", [128], F32, kind="Internal").ap()

    sc = Script()
    with ExitStack() as es:
        A = lambda n, sh, dt: es.enter_context(nc.sbuf_tensor(n, sh, dt))
        PS = lambda n, sh, dt: es.enter_context(nc.psum_tensor(n, sh, dt))
        stage_f = A("stage_f", [128, 1024], F32)
        wq_t = A("wq_t", [128, 1024], BF16)
        wuq = A("wuq", [128, 8 * I3], BF16)
        wgq = A("wgq", [128, 8 * I3], BF16)
        xq = A("xq", [128, 8 * T], BF16)
        ot = A("ot", [128, 512], BF16)
        awb = A("awb", [128, 32], F32)
        sbv = A("sbv", [128, 32], F32)
        rbv = A("rbv", [128, 32], F32)
        aw2 = A("aw2", [128, 1], F32)
        acc_u = A("acc_u", [128, 1], F32)
        acc_g = A("acc_g", [128, 1], F32)
        am8row = A("am8row", [1, 8], F32)
        am = A("am", [1, 1], F32)
        sc2 = A("sc2", [1, 2], F32)
        rsb = A("rsb", [128, 2], F32)
        amT = A("amT", [1, 128], F32)
        am1 = A("am1", [1, 1], F32)
        ident = A("ident", [128, 128], BF16)
        ps_m = PS("ps_m", [128, 512], F32)
        ps_tb = PS("ps_tb", [128, 128], BF16)

        add_identity(sc, ident)
        sc.c("vector", lambda v: v.memset(acc_u[:], 0.0))
        sc.c("vector", lambda v: v.memset(acc_g[:], 0.0))
        add_scalar_prep(sc, dict(am8row=am8row, am=am, sc2=sc2, rsb=rsb,
                                 scr2=scr2), amax8, 32767.0, 1e-12, "qdq")

        for db in range(8):
            for tc in range(8):
                sc.d("sync", lambda sync, db=db, tc=tc: sync.dma_start(
                    out=stage_f[:, 0:512], in_=x3T[ts(db, 128), ts(tc, 512)]))
                sc.c("vector", lambda v, db=db, tc=tc: v._custom_dve(
                    ROUND_SCALE, out=xq[:, db * T + tc * 512: db * T + tc * 512 + 512],
                    in0=stage_f[:, 0:512], s0=rsb[:, 0:1], s1=1.0, imm2=MAGIC))

        for w_hbm, wq_dst in ((w_up, wuq), (w_gate, wgq)):
            for rb in range(24):
                add_qdq_weight_lpbq(sc, nc, stage_f, wq_t, awb, sbv, rbv,
                                    w_hbm, rb, 32)
                for j in range(8):
                    sc.c("tensor", lambda te, j=j: te.transpose(
                        ps_tb[:], wq_t[:, ts(j, 128)], ident[:]))
                    sc.c("vector", lambda v, j=j, rb=rb, wq_dst=wq_dst: v.tensor_copy(
                        wq_dst[:, j * I3 + rb * 128: j * I3 + rb * 128 + 128],
                        ps_tb[:]))

        for wq_src, out_hbm, acc in ((wuq, upT, acc_u), (wgq, gateT, acc_g)):
            for ib in range(24):
                for tc in range(8):
                    def mmu(te, wq_src=wq_src, ib=ib, tc=tc):
                        ins = []
                        for db in range(8):
                            ins.append(te.matmul(
                                ps_m[:],
                                wq_src[:, db * I3 + ib * 128: db * I3 + ib * 128 + 128],
                                xq[:, db * T + tc * 512: db * T + tc * 512 + 512],
                                start=(db == 0), stop=(db == 7)))
                        return ins[-1]
                    sc.c("tensor", mmu)
                    sc.c("scalar", lambda s: s.activation(
                        out=ot[:], in_=ps_m[:], func=AF.Copy, scale=rsb[:, 1:2]))
                    sc.c("vector", lambda v: v.reduce_max(
                        aw2[:], ot[:], axis=AX.X, apply_absolute_value=True))
                    sc.c("vector", lambda v, acc=acc: v.tensor_tensor(
                        acc[:], acc[:], aw2[:], ALU.max))
                    sc.d("sync", lambda sync, out_hbm=out_hbm, ib=ib, tc=tc:
                         sync.dma_start(out=out_hbm[ts(ib, 128), ts(tc, 512)],
                                        in_=ot[:]))

        add_amax_finalize(sc, acc_u, amT, am1, am_bounce, amax_u)
        add_amax_finalize(sc, acc_g, amT, am1, am_bounce, amax_g)
        sc.run(nc)
    mybir.codegen_inst_isa_subclasses(nc)
    return nc



def build_l1_ffn_b():
    nc = bass.Bass("TRN2", target_bir_lowering=False, debug=False)
    gateT = nc.dram_tensor("gateT", [I3, T], BF16, kind="ExternalInput").ap()
    amax8 = nc.dram_tensor("amax8", [8, 1], F32, kind="ExternalInput").ap()
    mT = nc.dram_tensor("mT", [I3, T], BF16, kind="ExternalOutput").ap()
    amax_m = nc.dram_tensor("amax_m", [1, 1], F32, kind="ExternalOutput").ap()
    scr2 = nc.dram_tensor("scr2", [1, 2], F32, kind="Internal").ap()
    am_bounce = nc.dram_tensor("am_bounce", [128], F32, kind="Internal").ap()

    sc = Script()
    with ExitStack() as es:
        A = lambda n, sh, dt: es.enter_context(nc.sbuf_tensor(n, sh, dt))
        gt = A("gt", [128, 512], BF16)
        gq = A("gq", [128, 512], F32)
        sig = A("sig", [128, 512], F32)
        si2 = A("si2", [128, 512], F32)
        mm = A("mm_", [128, 512], F32)
        mb16 = A("mb16", [128, 512], BF16)
        aw2 = A("aw2", [128, 1], F32)
        macc = A("macc", [128, 1], F32)
        am8row = A("am8row", [1, 8], F32)
        am = A("am", [1, 1], F32)
        sc2 = A("sc2", [1, 2], F32)
        rsb = A("rsb", [128, 2], F32)
        amT = A("amT", [1, 128], F32)
        am1 = A("am1", [1, 1], F32)

        sc.c("vector", lambda v: v.memset(macc[:], 0.0))
        add_scalar_prep(sc, dict(am8row=am8row, am=am, sc2=sc2, rsb=rsb,
                                 scr2=scr2), amax8, 32767.0, 1e-12, "qdq")
        for ib in range(24):
            for tc in range(8):
                sc.d("sync", lambda sync, ib=ib, tc=tc: sync.dma_start(
                    out=gt[:], in_=gateT[ts(ib, 128), ts(tc, 512)]))
                sc.c("vector", lambda v: v._custom_dve(
                    ROUND_SCALE, out=gq[:], in0=gt[:], s0=rsb[:, 0:1],
                    s1=rsb[:, 1:2], imm2=MAGIC))
                sc.c("scalar", lambda s: s.activation(
                    out=sig[:], in_=gq[:], func=AF.Sigmoid))
                sc.c("vector", lambda v: v._custom_dve(
                    ROUND_SCALE, out=si2[:], in0=sig[:], s0=65536.0, s1=1.0,
                    imm2=MAGIC))
                sc.c("vector", lambda v: v.tensor_scalar(
                    si2[:], si2[:], 65535.0, 1.0 / 65536.0, ALU.min, ALU.mult))
                sc.c("vector", lambda v: v.tensor_tensor(
                    mm[:], gq[:], si2[:], ALU.mult))
                sc.c("scalar", lambda s: s.activation(
                    out=mb16[:], in_=mm[:], func=AF.Copy))
                sc.c("vector", lambda v: v.reduce_max(
                    aw2[:], mb16[:], axis=AX.X, apply_absolute_value=True))
                sc.c("vector", lambda v: v.tensor_tensor(
                    macc[:], macc[:], aw2[:], ALU.max))
                sc.d("sync", lambda sync, ib=ib, tc=tc: sync.dma_start(
                    out=mT[ts(ib, 128), ts(tc, 512)], in_=mb16[:]))
        add_amax_finalize(sc, macc, amT, am1, am_bounce, amax_m)
        sc.run(nc)
    mybir.codegen_inst_isa_subclasses(nc)
    return nc



def build_l1_ffn_c():
    nc = bass.Bass("TRN2", target_bir_lowering=False, debug=False)
    upT = nc.dram_tensor("upT", [I3, T], BF16, kind="ExternalInput").ap()
    mT = nc.dram_tensor("mT", [I3, T], BF16, kind="ExternalInput").ap()
    amax8u = nc.dram_tensor("amax8u", [8, 1], F32, kind="ExternalInput").ap()
    amax8m = nc.dram_tensor("amax8m", [8, 1], F32, kind="ExternalInput").ap()
    oT = nc.dram_tensor("oT", [I3, T], BF16, kind="ExternalOutput").ap()
    amax_o = nc.dram_tensor("amax_o", [1, 1], F32, kind="ExternalOutput").ap()
    scr2 = nc.dram_tensor("scr2", [1, 2], F32, kind="Internal").ap()
    scr2b = nc.dram_tensor("scr2b", [1, 2], F32, kind="Internal").ap()
    am_bounce = nc.dram_tensor("am_bounce", [128], F32, kind="Internal").ap()

    sc = Script()
    with ExitStack() as es:
        A = lambda n, sh, dt: es.enter_context(nc.sbuf_tensor(n, sh, dt))
        ut = A("ut", [128, 512], BF16)
        mt = A("mt", [128, 512], BF16)
        uq = A("uq", [128, 512], F32)
        mq = A("mq", [128, 512], F32)
        oo = A("oo", [128, 512], F32)
        ob16 = A("ob16", [128, 512], BF16)
        aw2 = A("aw2", [128, 1], F32)
        oacc = A("oacc", [128, 1], F32)
        am8row = A("am8row", [1, 8], F32)
        am = A("am", [1, 1], F32)
        sc2 = A("sc2", [1, 2], F32)
        rsbu = A("rsbu", [128, 2], F32)
        rsbm = A("rsbm", [128, 2], F32)
        amT = A("amT", [1, 128], F32)
        am1 = A("am1", [1, 1], F32)

        sc.c("vector", lambda v: v.memset(oacc[:], 0.0))
        add_scalar_prep(sc, dict(am8row=am8row, am=am, sc2=sc2, rsb=rsbu,
                                 scr2=scr2), amax8u, 32767.0, 1e-12, "qdq")
        add_scalar_prep(sc, dict(am8row=am8row, am=am, sc2=sc2, rsb=rsbm,
                                 scr2=scr2b), amax8m, 32767.0, 1e-12, "qdq")
        for ib in range(24):
            for tc in range(8):
                sc.d("sync", lambda sync, ib=ib, tc=tc: [
                    sync.dma_start(out=ut[:], in_=upT[ts(ib, 128), ts(tc, 512)]),
                    sync.dma_start(out=mt[:], in_=mT[ts(ib, 128), ts(tc, 512)])],
                    n=2)
                sc.c("vector", lambda v: v._custom_dve(
                    ROUND_SCALE, out=uq[:], in0=ut[:], s0=rsbu[:, 0:1],
                    s1=rsbu[:, 1:2], imm2=MAGIC))
                sc.c("vector", lambda v: v._custom_dve(
                    ROUND_SCALE, out=mq[:], in0=mt[:], s0=rsbm[:, 0:1],
                    s1=rsbm[:, 1:2], imm2=MAGIC))
                sc.c("vector", lambda v: v.tensor_tensor(
                    oo[:], uq[:], mq[:], ALU.mult))
                sc.c("scalar", lambda s: s.activation(
                    out=ob16[:], in_=oo[:], func=AF.Copy))
                sc.c("vector", lambda v: v.reduce_max(
                    aw2[:], ob16[:], axis=AX.X, apply_absolute_value=True))
                sc.c("vector", lambda v: v.tensor_tensor(
                    oacc[:], oacc[:], aw2[:], ALU.max))
                sc.d("sync", lambda sync, ib=ib, tc=tc: sync.dma_start(
                    out=oT[ts(ib, 128), ts(tc, 512)], in_=ob16[:]))
        add_amax_finalize(sc, oacc, amT, am1, am_bounce, amax_o)
        sc.run(nc)
    mybir.codegen_inst_isa_subclasses(nc)
    return nc



def build_l1_ffn_d():
    nc = bass.Bass("TRN2", target_bir_lowering=False, debug=False)
    x3T = nc.dram_tensor("x3T", [D, T], F32, kind="ExternalInput").ap()
    oT = nc.dram_tensor("oT", [I3, T], BF16, kind="ExternalInput").ap()
    amax8 = nc.dram_tensor("amax8", [8, 1], F32, kind="ExternalInput").ap()
    w_down = nc.dram_tensor("w_down", [D, I3], F32, kind="ExternalInput").ap()
    x4T = nc.dram_tensor("x4T", [D, T], F32, kind="ExternalOutput").ap()
    scr2 = nc.dram_tensor("scr2", [1, 2], F32, kind="Internal").ap()

    sc = Script()
    with ExitStack() as es:
        A = lambda n, sh, dt: es.enter_context(nc.sbuf_tensor(n, sh, dt))
        PS = lambda n, sh, dt: es.enter_context(nc.psum_tensor(n, sh, dt))
        stage_f = A("stage_f", [128, 3072], F32)
        wq_t = A("wq_t", [128, 3072], BF16)
        wdq = A("wdq", [128, 24 * 1024], BF16)
        oqc = A("oqc", [128, 24 * 512], BF16)
        yr = A("yr", [128, 8 * 512], F32)
        xtile = A("xtile", [128, 512], F32)
        sq = A("sq", [128, 512], F32)
        x4t = A("x4t", [128, 512], F32)
        tprod = A("tprod", [128, 512], F32)
        awb = A("awb", [128, 96], F32)
        sbv = A("sbv", [128, 96], F32)
        rbv = A("rbv", [128, 96], F32)
        am8row = A("am8row", [1, 8], F32)
        am = A("am", [1, 1], F32)
        sc2 = A("sc2", [1, 2], F32)
        rsb = A("rsb", [128, 2], F32)
        lnrows = A("lnrows", [1, 512 * 6], F32)
        ones_col = A("ones_col", [128, 1], F32)
        ones_row1 = A("ones_row1", [1, 128], F32)
        cpool = A("cpool", [128, 1], F32)
        ident = A("ident", [128, 128], BF16)
        ps_m = PS("ps_m", [128, 512], F32)
        ps_tb = PS("ps_tb", [128, 128], BF16)
        ps_sum = PS("ps_sum", [1, 512], F32)
        ps_sq = PS("ps_sq", [1, 512], F32)
        ps_al = PS("ps_al", [128, 512], F32)
        ps_be = PS("ps_be", [128, 512], F32)

        add_const_pool(nc, sc, cpool, [1e-5])
        add_identity(sc, ident)
        sc.c("vector", lambda v: v.memset(ones_col[:], 1.0))
        sc.c("vector", lambda v: v.memset(ones_row1[:], 1.0))
        add_scalar_prep(sc, dict(am8row=am8row, am=am, sc2=sc2, rsb=rsb,
                                 scr2=scr2), amax8, 32767.0, 1e-12, "qdq")

        for rb in range(8):
            add_qdq_weight_lpbq(sc, nc, stage_f, wq_t, awb, sbv, rbv,
                                w_down, rb, 96)
            for j in range(24):
                sc.c("tensor", lambda te, j=j: te.transpose(
                    ps_tb[:], wq_t[:, ts(j, 128)], ident[:]))
                sc.c("vector", lambda v, j=j, rb=rb: v.tensor_copy(
                    wdq[:, j * 1024 + rb * 128: j * 1024 + rb * 128 + 128],
                    ps_tb[:]))

        for tc in range(8):
            for ib in range(24):
                sc.d("sync", lambda sync, ib=ib, tc=tc: sync.dma_start(
                    out=oqc[:, ts(ib, 512)], in_=oT[ts(ib, 128), ts(tc, 512)]))
                sc.c("vector", lambda v, ib=ib: v._custom_dve(
                    ROUND_SCALE, out=oqc[:, ts(ib, 512)], in0=oqc[:, ts(ib, 512)],
                    s0=rsb[:, 0:1], s1=rsb[:, 1:2], imm2=MAGIC))
            for do in range(8):
                def mmo(te, do=do):
                    ins = []
                    for ib in range(24):
                        ins.append(te.matmul(
                            ps_m[:],
                            wdq[:, ib * 1024 + do * 128: ib * 1024 + do * 128 + 128],
                            oqc[:, ts(ib, 512)],
                            start=(ib == 0), stop=(ib == 23)))
                    return ins[-1]
                sc.c("tensor", mmo)
                sc.d("sync", lambda sync, do=do, tc=tc: sync.dma_start(
                    out=xtile[:], in_=x3T[ts(do, 128), ts(tc, 512)]))
                sc.c("vector", lambda v, do=do: v.scalar_tensor_tensor(
                    yr[:, ts(do, 512)], ps_m[:], 1.0, xtile[:],
                    ALU.mult, ALU.add))
                sc.c("scalar", lambda s, do=do: s.activation(
                    out=sq[:], in_=yr[:, ts(do, 512)], func=AF.Square))

                def mmstat(te, do=do):
                    te.matmul(ps_sum[:], ones_col[:], yr[:, ts(do, 512)],
                              start=(do == 0), stop=(do == 7))
                    return te.matmul(ps_sq[:], ones_col[:], sq[:],
                                     start=(do == 0), stop=(do == 7))
                sc.c("tensor", mmstat)
            add_ln_rows(sc, lnrows, ps_sum, ps_sq, ps_al, ps_be, ones_row1,
                        1.0 / D, 1e-5)
            for do in range(8):
                sc.c("vector", lambda v, do=do: v.tensor_tensor(
                    tprod[:], yr[:, ts(do, 512)], ps_al[:], ALU.mult))
                sc.c("vector", lambda v: v.tensor_tensor(
                    x4t[:], tprod[:], ps_be[:], ALU.add))
                sc.d("sync", lambda sync, do=do, tc=tc: sync.dma_start(
                    out=x4T[ts(do, 128), ts(tc, 512)], in_=x4t[:]))
        sc.run(nc)
    mybir.codegen_inst_isa_subclasses(nc)
    return nc



_NC_CACHE = {}


def _get_nc(name, builder):
    if name not in _NC_CACHE:
        _NC_CACHE[name] = builder()
    return _NC_CACHE[name]


def _gather_amax_n(res, key, n):
    return np.ascontiguousarray(np.concatenate(
        [np.asarray(res.results[c][key]).reshape(1, 1) for c in range(n)]
        + [np.asarray(res.results[0][key]).reshape(1, 1)] * (8 - n),
        axis=0)).astype(np.float32)


def _gather_amax(res, key):
    return _gather_amax_n(res, key, NCORES)


def kernel(L0_attn_x, L0_attn_W_qkv, L0_ffn_W_in, L0_ffn_b_in, L0_ffn_W_out,
           L0_ffn_b_out, L1_attn_A_log, L1_attn_W_delta, L1_attn_b_delta,
           L1_attn_W_B, L1_attn_W_C, L1_attn_D_skip, L1_attn_norm_weight,
           L1_ffn_w_gate, L1_ffn_w_up, L1_ffn_w_down):
    cores = list(range(NCORES))
    x = np.ascontiguousarray(np.asarray(L0_attn_x, dtype=np.float32))
    wqkv = np.ascontiguousarray(np.asarray(L0_attn_W_qkv, np.float32))
    w_inT = np.ascontiguousarray(np.asarray(L0_ffn_W_in, np.float32).T)
    b_in_t = np.ascontiguousarray(np.asarray(L0_ffn_b_in, np.float32).reshape(32, 128).T)
    w_out = np.ascontiguousarray(np.asarray(L0_ffn_W_out, np.float32))
    b_out_t = np.ascontiguousarray(np.asarray(L0_ffn_b_out, np.float32).reshape(8, 128).T)
    a_log = np.ascontiguousarray(np.asarray(L1_attn_A_log, np.float32))
    w_delta = np.ascontiguousarray(np.asarray(L1_attn_W_delta, np.float32))
    b_delta_t = np.ascontiguousarray(np.asarray(L1_attn_b_delta, np.float32).reshape(8, 128).T)
    w_b = np.ascontiguousarray(np.asarray(L1_attn_W_B, np.float32))
    w_c = np.ascontiguousarray(np.asarray(L1_attn_W_C, np.float32))
    dskip_t = np.ascontiguousarray(np.asarray(L1_attn_D_skip, np.float32).reshape(8, 128).T)
    nw_t = np.ascontiguousarray(np.asarray(L1_attn_norm_weight, np.float32).reshape(8, 128).T)
    w_gate = np.ascontiguousarray(np.asarray(L1_ffn_w_gate, np.float32))
    w_up = np.ascontiguousarray(np.asarray(L1_ffn_w_up, np.float32))
    w_down = np.ascontiguousarray(np.asarray(L1_ffn_w_down, np.float32))

    nc1 = _get_nc("l0_attn", build_l0_attn)
    r1 = run_bass_kernel_spmd(nc1, [
        {"x_tok": np.ascontiguousarray(x[c]), "wqkv": wqkv} for c in cores],
        core_ids=cores)
    x1 = [np.asarray(r1.results[c]["x1T"]) for c in cores]
    am1 = _gather_amax(r1, "amax1")

    nc2 = _get_nc("l0_ffn_a", build_l0_ffn_a)
    r2 = run_bass_kernel_spmd(nc2, [
        {"x1T": x1[c], "amax8": am1, "w_inT": w_inT, "b_in_t": b_in_t}
        for c in cores], core_ids=cores)
    hid = [np.asarray(r2.results[c]["hidT"]) for c in cores]
    amh = _gather_amax(r2, "amax_h")

    nc3 = _get_nc("l0_ffn_b", build_l0_ffn_b)
    r3 = run_bass_kernel_spmd(nc3, [
        {"x1T": x1[c], "hidT": hid[c], "amax8": amh, "w_out": w_out,
         "b_out_t": b_out_t} for c in cores], core_ids=cores)
    x2 = [np.asarray(r3.results[c]["x2T"]) for c in cores]

    nc4 = _get_nc("ssm", build_ssm)
    r4 = run_bass_kernel_spmd(nc4, [
        {"x2T": x2[c], "a_log": a_log, "w_delta": w_delta,
         "b_delta_t": b_delta_t, "w_b": w_b, "w_c": w_c, "dskip_t": dskip_t,
         "nw_t": nw_t} for c in cores], core_ids=cores)
    x3 = [np.asarray(r4.results[c]["x3T"]) for c in cores]
    am3 = _gather_amax(r4, "amax3")

    nc5 = _get_nc("l1_ffn_a", build_l1_ffn_a)
    r5 = run_bass_kernel_spmd(nc5, [
        {"x3T": x3[c], "amax8": am3, "w_up": w_up, "w_gate": w_gate}
        for c in cores], core_ids=cores)
    up = [np.asarray(r5.results[c]["upT"]) for c in cores]
    gate = [np.asarray(r5.results[c]["gateT"]) for c in cores]
    amu = _gather_amax(r5, "amax_u")
    amg = _gather_amax(r5, "amax_g")

    nc6 = _get_nc("l1_ffn_b", build_l1_ffn_b)
    r6 = run_bass_kernel_spmd(nc6, [
        {"gateT": gate[c], "amax8": amg} for c in cores], core_ids=cores)
    mm_ = [np.asarray(r6.results[c]["mT"]) for c in cores]
    amm = _gather_amax(r6, "amax_m")

    nc7 = _get_nc("l1_ffn_c", build_l1_ffn_c)
    r7 = run_bass_kernel_spmd(nc7, [
        {"upT": up[c], "mT": mm_[c], "amax8u": amu, "amax8m": amm}
        for c in cores], core_ids=cores)
    oo = [np.asarray(r7.results[c]["oT"]) for c in cores]
    amo = _gather_amax(r7, "amax_o")

    nc8 = _get_nc("l1_ffn_d", build_l1_ffn_d)
    r8 = run_bass_kernel_spmd(nc8, [
        {"x3T": x3[c], "oT": oo[c], "amax8": amo, "w_down": w_down}
        for c in cores], core_ids=cores)
    out = np.stack([np.ascontiguousarray(np.asarray(r8.results[c]["x4T"]).T)
                    for c in cores])
    return out.astype(np.float32)
